# version 07 (final)
# baseline (speedup 1.0000x reference)
import contextlib
import numpy as np
import concourse.bass as bass
import concourse.mybir as mybir
from concourse.bass_utils import run_bass_kernel_spmd

F32 = mybir.dt.float32
BF16 = mybir.dt.bfloat16
AF = mybir.ActivationFunctionType
ALU = mybir.AluOpType

D = 1024
DI = 2048
NST = 16
RANK = 64
DH = 4096
L = 4096
TC = 512
NCH = L // TC
NDT = DI // 128
NSTG = 12
NCORES = 8
REC = float(np.float32(1.0 / D))
EPS = 1e-5


class Ctr:
    def __init__(self, ctx, nc, name):
        self.h = ctx.enter_context(nc.semaphore(name))
        self._nm = name
        self.n = 0
        self.maxwait = 0
        self.spwait = 0


def build(taps=(), phases=(0, 1, 2, 3, 4), noaccum=False):
    nc = bass.Bass("TRN2", target_bir_lowering=False,
                   detect_race_conditions=False)
    PROG = []
    CURPHASE = [-1]

    def on(e, fn):
        p = CURPHASE[0]
        if p == -1 or p in phases:
            PROG.append((e, fn))
            return
        while p > 4:
            p //= 10
        if p in phases:
            PROG.append((e, fn))

    def din(name, shape, dtype=F32):
        return nc.dram_tensor(name, shape, dtype, kind="ExternalInput").ap()

    x_hbm = din("x", [L, D])
    wn_rep = din("wn_rep", [128, D])
    bn_rep = din("bn_rep", [128, D])
    winT = din("winT", [D, 2 * DI])
    woutT = din("woutT", [DI, D])
    wexpT = din("wexpT", [DH, D])
    wcon = din("wcon", [DH, D])
    upS_row = din("upS_row", [NSTG, DH])
    upX_row = din("upX_row", [NSTG, DH])
    dnS_row = din("dnS_row", [NSTG, DH])
    dnXT_row = din("dnXT_row", [NSTG, DH])
    upS_col = din("upS_col", [NSTG, 128, 32])
    upX_col = din("upX_col", [NSTG, 128, 32])
    dnS_col = din("dnS_col", [NSTG, 128, 32])
    dnXT_col = din("dnXT_col", [NSTG, 128, 32])
    bexp_row = din("bexp_row", [1, DH])
    upb_col = din("upb_col", [128, 32])
    bcon_col = din("bcon_col", [128, 8])
    dnb_col = din("dnb_col", [128, 8])
    A_tab, cw_tab, cb_tab, dtb_tab, Dp_tab, xpwT_in, dtwT_in = \
        ({} for _ in range(7))
    for s in ("f", "r"):
        A_tab[s] = din(f"A_tab_{s}", [128, NDT * NST])
        cw_tab[s] = din(f"cw_tab_{s}", [128, NDT * 4])
        cb_tab[s] = din(f"cb_tab_{s}", [128, NDT])
        dtb_tab[s] = din(f"dtb_tab_{s}", [128, NDT])
        Dp_tab[s] = din(f"Dp_tab_{s}", [128, NDT])
        xpwT_in[s] = din(f"xpwT_{s}", [DI, 96])
        dtwT_in[s] = din(f"dtwT_{s}", [RANK, DI])
    out_hbm = nc.dram_tensor("out", [L, D], F32, kind="ExternalOutput").ap()

    def scr(name, shape, dtype):
        kind = "ExternalOutput" if name in taps else "Internal"
        return nc.dram_tensor(name, shape, dtype, kind=kind).ap()

    xz_sp = scr("xz_sp", [2 * DI, L], F32)
    xn_sp = scr("xn_sp", [L, D], F32)
    xnfm_sp = scr("xnfm_sp", [D, L], BF16)
    yf_sp = scr("yf_sp", [DI, L], F32)
    attn_sp = scr("attn_sp", [L, D], F32)
    bc_sp = scr("bc_sp", [2, 2 * NST, TC], F32)
    dbg_on = ("ysum_sp" in taps) or ("yr_sp" in taps)
    yr_sp = scr("yr_sp", [DI, L] if dbg_on else [1, 1], F32)
    ysum_sp = scr("ysum_sp", [DI, L] if dbg_on else [1, 1], F32)
    beff_sp = scr("beff_sp", [1, DH], F32)
    weff_sp = scr("weff_sp", [D, DH], BF16)
    wcon_sp = scr("wcon_sp", [DH, D], BF16)

    with contextlib.ExitStack() as CTX:
        def sbuf(name, shape, dtype, ctx=None):
            return (ctx or CTX).enter_context(
                nc.sbuf_tensor(name, shape, dtype))

        cLD = Ctr(CTX, nc, "cLD")
        cSP = Ctr(CTX, nc, "cSP")
        cBC = Ctr(CTX, nc, "cBC")
        sv = Ctr(CTX, nc, "sv")
        sa = Ctr(CTX, nc, "sa")
        st_ = Ctr(CTX, nc, "st")
        sg = Ctr(CTX, nc, "sg")

        def dma(chan, out_ap, in_ap, noncontig=False):
            if chan.maxwait > chan.spwait:
                v0 = chan.maxwait
                on("y", lambda e, h=chan.h, v=v0: e.wait_ge(h, v))
                chan.spwait = v0
            chan.n += 16
            if noncontig:
                def f(e, o=out_ap, i=in_ap, c=chan):
                    with nc.allow_non_contiguous_dma(reason="tiny reshape"):
                        e.dma_start(out=o, in_=i).then_inc(c.h, 16)
            else:
                def f(e, o=out_ap, i=in_ap, c=chan):
                    e.dma_start(out=o, in_=i).then_inc(c.h, 16)
            on("y", f)
            return chan.n

        def wt(ek, cm, val):
            if val is None or val <= 0:
                return
            WAITMAP.append((ek, getattr(cm, "_nm", "?"), val,
                            _sys._getframe(1).f_lineno))
            if hasattr(cm, "maxwait"):
                cm.maxwait = max(cm.maxwait, val)
                if ek == "y":
                    cm.spwait = max(cm.spwait, val)
            on(ek, lambda eng, h=cm.h, v=val: eng.wait_ge(h, v))

        import sys as _sys
        MARKMAP = {}
        WAITMAP = []
        nc._markmap = MARKMAP
        nc._waitmap = WAITMAP

        def _mk(key, mile):
            def emit(fn, mark=False):
                if mark:
                    mile.n += 1
                    MARKMAP[(key, mile.n)] = _sys._getframe(1).f_lineno
                    on(key, lambda e, fn=fn: fn(e).then_inc(mile.h, 1))
                    return mile.n
                on(key, lambda e, fn=fn: fn(e))
                return None
            return emit
        act = _mk("a", sa)
        vec = _mk("v", sv)
        pe = _mk("p", st_)

        def barrier():
            sa.n += 1
            on("a", lambda a: a.copy(mrk_sb[0:1, 0:1],
                                     mrk_sb[0:1, 1:2]).then_inc(sa.h, 1))
            sv.n += 1
            on("v", lambda e: e.engine_nop().then_inc(sv.h, 1))
            st_.n += 1
            on("p", lambda e: e.nop().then_inc(st_.h, 1))
            sg.n += 1
            on("g", lambda e: e.engine_nop().then_inc(sg.h, 1))
            waits = [(sa, sa.n), (sv, sv.n), (st_, st_.n), (sg, sg.n),
                     (cLD, cLD.n), (cSP, cSP.n), (cBC, cBC.n)]
            for ek in ("a", "v", "p", "g", "y"):
                for m, v in waits:
                    wt(ek, m, v)

        ident = sbuf("ident", [128, 128], BF16)
        mrk_sb = sbuf("mrk_sb", [128, 4], F32)
        ident_f = sbuf("ident_f", [128, 128], F32)
        A_sb = {s: sbuf(f"A_sb_{s}", [128, NDT * NST], F32) for s in "fr"}
        cw_sb = {s: sbuf(f"cw_sb_{s}", [128, NDT * 4], F32) for s in "fr"}
        cb_sb = {s: sbuf(f"cb_sb_{s}", [128, NDT], F32) for s in "fr"}
        dtb_sb = {s: sbuf(f"dtb_sb_{s}", [128, NDT], F32) for s in "fr"}
        Dp_sb = {s: sbuf(f"Dp_sb_{s}", [128, NDT], F32) for s in "fr"}
        bias1_sb = sbuf("bias1_sb", [128, 32], F32)
        bias2_sb = sbuf("bias2_sb", [128, 8], F32)
        carry_sb = sbuf("carry_sb", [128, NDT * NST], F32)
        psum = [CTX.enter_context(nc.psum_tensor(f"pb{i}", [128, 512], F32))
                for i in range(6)]
        psum_bf = [CTX.enter_context(
            nc.psum_tensor(f"pbf{i}", [128, 1024], BF16)) for i in range(2)]

        sg_base = sg.n

        def g_init(e, b=sg_base):
            e.memset(ident[:], 1.0).then_inc(sg.h, 1)
            e.wait_ge(sg.h, b + 1)
            e.affine_select(ident[:], ident[:], [[1, 128]], ALU.is_equal,
                            0.0, base=0,
                            channel_multiplier=-1).then_inc(sg.h, 1)
            e.memset(ident_f[:], 1.0).then_inc(sg.h, 1)
            e.wait_ge(sg.h, b + 3)
            e.affine_select(ident_f[:], ident_f[:], [[1, 128]], ALU.is_equal,
                            0.0, base=0,
                            channel_multiplier=-1).then_inc(sg.h, 1)
        on("g", g_init)
        sg.n += 4
        v_ident = sg.n
        vec(lambda e: e.memset(mrk_sb[:], 0.0))

        for s in "fr":
            dma(cLD, A_sb[s][:], A_tab[s])
            dma(cLD, cw_sb[s][:], cw_tab[s])
            dma(cLD, cb_sb[s][:], cb_tab[s])
            dma(cLD, dtb_sb[s][:], dtb_tab[s])
            dma(cLD, Dp_sb[s][:], Dp_tab[s])
        wt("a", cLD, cLD.n)
        act(lambda a, t=A_sb["f"]: a.activation(t[:], t[:], AF.Exp))
        va = act(lambda a, t=A_sb["r"]: a.activation(t[:], t[:], AF.Exp),
                 mark=True)
        wt("v", sa, va)
        for s in "fr":
            vec(lambda e, t=A_sb[s]: e.tensor_scalar_mul(t[:], t[:], -1.0))
        b2t = sbuf("b2t", [128, 8], F32)
        dma(cLD, bias2_sb[:], bcon_col)
        lv = dma(cLD, b2t[:], dnb_col)
        wt("v", cLD, lv)
        vec(lambda e: e.tensor_add(bias2_sb[:], bias2_sb[:], b2t[:]))

        CURPHASE[0] = 0
        with contextlib.ExitStack() as P0:

            M_sb = sbuf("M_sb", [128, 32 * 128], F32, P0)
            M2_sb = sbuf("M2_sb", [128, 32 * 128], F32, P0)
            Mbf_sb = sbuf("Mbf_sb", [128, 32 * 128], BF16, P0)
            cS_sb = sbuf("cS_sb", [128, DH], F32, P0)
            cX_sb = sbuf("cX_sb", [128, DH], F32, P0)
            ctmp_sb = sbuf("ctmp_sb", [128, DH], F32, P0)
            wbig = sbuf("wbig", [128, 32 * 512], F32, P0)
            wld = sbuf("wld", [128, 2 * 512], F32, P0)
            wldb = sbuf("wldb", [128, 2 * 512], BF16, P0)
            cols = sbuf("cols", [128, 64], F32, P0)
            ptmp = sbuf("ptmp", [128, 4 * 512], F32, P0)
            ebuf = sbuf("ebuf", [128, 512], BF16, P0)

            wt("p", sg, v_ident)

            def compose_M(S_row, X_row, order):
                guard_v = sv.n
                guard_p = st_.n
                b = sg.n

                def ginit(e, guard_v=guard_v, b=b):
                    e.wait_ge(sv.h, guard_v)
                    e.memset(M_sb[:], 1.0).then_inc(sg.h, 1)
                    e.wait_ge(sg.h, b + 1)
                    e.affine_select(M_sb[:], M_sb[:], [[0, 32], [1, 128]],
                                    ALU.is_equal, 0.0, base=0,
                                    channel_multiplier=-1).then_inc(sg.h, 1)
                on("g", ginit)
                sg.n += 2
                wt("v", sg, sg.n)
                cur, nxt = M_sb, M2_sb
                coeff_used = 0
                for s in order:
                    stp = 2 ** s
                    wt("y", sv, coeff_used)
                    dma(cLD, cS_sb[:],
                        S_row[s:s + 1, :].partition_broadcast(128))
                    l1 = dma(cLD, cX_sb[:],
                             X_row[s:s + 1, :].partition_broadcast(128))
                    wt("v", cLD, l1)
                    shp = dict(hf=2, of=stp)
                    curv = cur.ap().rearrange(
                        "p (gb hf of) -> p gb hf of", **shp)
                    curp = curv[:, :, ::-1, :]
                    nxtv = nxt.ap().rearrange(
                        "p (gb hf of) -> p gb hf of", **shp)
                    cSv = cS_sb.ap().rearrange(
                        "p (gb hf of) -> p gb hf of", **shp)
                    cXv = cX_sb.ap().rearrange(
                        "p (gb hf of) -> p gb hf of", **shp)
                    vec(lambda e, a=nxtv, b=curv, c=cSv: e.tensor_tensor(
                        a, b, c, ALU.mult))
                    vec(lambda e, b=curp, c=cXv, t=ctmp_sb: e.tensor_tensor(
                        t[:].rearrange("p (gb hf of) -> p gb hf of", **shp),
                        b, c, ALU.mult))
                    coeff_used = vec(lambda e, a=nxt, t=ctmp_sb: e.tensor_add(
                        a[:], a[:], t[:]), mark=True)
                    cur, nxt = nxt, cur
                wt("v", st_, guard_p)
                vec(lambda e, c=cur: e.tensor_copy(Mbf_sb[:], c[:]))
                return vec(lambda e: e.engine_nop(), mark=True)

            def bhigh_pairs(S_col, X_col, stage_list, wbuf, ncols):
                W3 = wbuf.ap().rearrange("p (g c) -> p g c", c=ncols)
                cols_used = [0]
                last = None
                for s in stage_list:
                    m = 2 ** (s - 7)
                    wt("y", sa, cols_used[0])
                    dma(cLD, cols[:, 0:32], S_col[s])
                    l1 = dma(cLD, cols[:, 32:64], X_col[s])
                    wt("a", cLD, l1)
                    for g in range(32):
                        gm = g ^ m
                        if gm < g:
                            continue
                        act(lambda a, g=g: a.activation(
                            ptmp[:, 0:ncols], W3[:, g, :], AF.Identity,
                            scale=cols[:, g:g + 1]))
                        act(lambda a, g=g, gm=gm: a.activation(
                            ptmp[:, 512:512 + ncols], W3[:, gm, :], AF.Identity,
                            scale=cols[:, 32 + g:33 + g]))
                        act(lambda a, gm=gm: a.activation(
                            ptmp[:, 1024:1024 + ncols], W3[:, gm, :], AF.Identity,
                            scale=cols[:, gm:gm + 1]))
                        vaa = act(lambda a, g=g, gm=gm: a.activation(
                            ptmp[:, 1536:1536 + ncols], W3[:, g, :], AF.Identity,
                            scale=cols[:, 32 + gm:33 + gm]), mark=True)
                        cols_used[0] = vaa
                        wt("v", sa, vaa)
                        vec(lambda e, g=g: e.tensor_add(
                            W3[:, g, :], ptmp[:, 0:ncols],
                            ptmp[:, 512:512 + ncols]))
                        last = vec(lambda e, gm=gm: e.tensor_add(
                            W3[:, gm, :], ptmp[:, 1024:1024 + ncols],
                            ptmp[:, 1536:1536 + ncols]), mark=True)
                        wt("a", sv, last)
                return vec(lambda e: e.engine_nop(), mark=True)

            vm = compose_M(upS_row, upX_row, range(7))
            wt("p", sv, vm)
            wldslot = [0, 0]
            for half in range(2):
                d0 = half * 512
                for g in range(32):
                    slot = g % 2
                    wt("y", sv, wldslot[slot])
                    lvv = dma(cLD, wld[:, slot * 512:(slot + 1) * 512],
                              wexpT[g * 128:(g + 1) * 128, d0:d0 + 512])
                    wt("v", cLD, lvv)
                    vv = vec(lambda e, slot=slot: e.tensor_copy(
                        wldb[:, slot * 512:(slot + 1) * 512],
                        wld[:, slot * 512:(slot + 1) * 512]), mark=True)
                    wldslot[slot] = vv
                    wt("p", sv, vv)
                    bk = psum[g % 2]
                    vmk = pe(lambda e, g=g, slot=slot, bk=bk:
                             nc.tensor.matmul(
                                 bk[:, 0:512],
                                 Mbf_sb[:, g * 128:(g + 1) * 128],
                                 wldb[:, slot * 512:(slot + 1) * 512],
                                 start=True, stop=True), mark=True)
                    wldslot[slot] = max(wldslot[slot], 0)
                    wt("a", st_, vmk)
                    vaa = act(lambda a, g=g, bk=bk: a.copy(
                        wbig[:, g * 512:(g + 1) * 512], bk[:, 0:512]),
                        mark=True)
                    wt("p", sa, vaa)
                vm2 = bhigh_pairs(upS_col, upX_col, range(7, 12), wbig, 512)
                wt("a", sv, vm2)
                for g in range(32):
                    vaa = act(lambda a, g=g: a.copy(
                        wldb[:, 0:512], wbig[:, g * 512:(g + 1) * 512]),
                        mark=True)
                    wt("p", sa, vaa)
                    va2 = None
                    for jb in range(4):
                        bk = psum_bf[jb % 2]
                        vmk = pe(lambda e, jb=jb, bk=bk: nc.tensor.transpose(
                            bk[:, 0:128], wldb[:, jb * 128:(jb + 1) * 128],
                            ident[:]), mark=True)
                        wt("a", st_, vmk)
                        va2 = act(lambda a, jb=jb, bk=bk: a.copy(
                            ebuf[:, jb * 128:(jb + 1) * 128], bk[:, 0:128]),
                            mark=True)
                        wt("p", sa, va2)
                    wt("y", sa, va2)
                    spv = dma(
                        cSP,
                        weff_sp[d0:d0 + 512, g * 128:(g + 1) * 128].rearrange(
                            "(jb p) f -> p jb f", p=128),
                        ebuf.ap().rearrange("p (jb f) -> p jb f", f=128))
                    wt("a", cSP, spv)

            dma(cLD, ctmp_sb[0:1, 0:DH], bexp_row)
            vbe = 0
            for s in range(NSTG):
                stp = 2 ** s
                wt("y", sv, vbe)
                dma(cLD, cS_sb[0:1, 0:DH], upS_row[s:s + 1, :])
                l1 = dma(cLD, cX_sb[0:1, 0:DH], upX_row[s:s + 1, :])
                wt("v", cLD, l1)
                shp = dict(hf=2, of=stp)
                bv = ctmp_sb[0:1, 0:DH].rearrange(
                    "p (gb hf of) -> p gb hf of", **shp)
                bp = bv[:, :, ::-1, :]
                sS = cS_sb[0:1, 0:DH].rearrange(
                    "p (gb hf of) -> p gb hf of", **shp)
                sX = cX_sb[0:1, 0:DH].rearrange(
                    "p (gb hf of) -> p gb hf of", **shp)
                tv = wbig[0:1, 0:DH].rearrange(
                    "p (gb hf of) -> p gb hf of", **shp)
                vec(lambda e, bp=bp, sX=sX, tv=tv: e.tensor_tensor(
                    tv, bp, sX, ALU.mult))
                vec(lambda e, bv=bv, sS=sS: e.tensor_tensor(
                    bv, bv, sS, ALU.mult))
                vbe = vec(lambda e: e.tensor_add(
                    ctmp_sb[0:1, 0:DH], ctmp_sb[0:1, 0:DH],
                    wbig[0:1, 0:DH]), mark=True)
            wt("y", sv, vbe)
            spv = dma(cSP, beff_sp, ctmp_sb[0:1, 0:DH])
            wt("y", cSP, spv)
            dma(cLD, bias1_sb[:],
                beff_sp.rearrange("o (g p) -> (o p) g", p=128),
                noncontig=True)
            lv8 = dma(cLD, cols[:, 0:32], upb_col)
            wt("v", cLD, lv8)
            vec(lambda e: e.tensor_add(bias1_sb[:], bias1_sb[:],
                                       cols[:, 0:32]))

            vm = compose_M(dnS_row, dnXT_row, range(6, -1, -1))
            wt("p", sv, vm)
            dn_used_v, dn_used_a = 0, 0
            for half in range(2):
                d0 = half * 512
                lvv = None
                wt("y", sv, dn_used_v)
                wt("y", sa, dn_used_a)
                for g in range(32):
                    lvv = dma(cLD, wbig[:, g * 512:(g + 1) * 512],
                              wcon[g * 128:(g + 1) * 128, d0:d0 + 512])
                wt("a", cLD, lvv)
                wt("v", cLD, lvv)
                vm2 = bhigh_pairs(dnS_col, dnXT_col, range(11, 6, -1),
                                  wbig, 512)
                wt("v", sv, vm2)
                for g in range(32):
                    slot = g % 2
                    vv = vec(lambda e, g=g, slot=slot: e.tensor_copy(
                        wldb[:, slot * 512:(slot + 1) * 512],
                        wbig[:, g * 512:(g + 1) * 512]), mark=True)
                    dn_used_v = vv
                    wt("p", sv, vv)
                    bk = psum[g % 2]
                    vmk = pe(lambda e, g=g, slot=slot, bk=bk:
                             nc.tensor.matmul(
                                 bk[:, 0:512],
                                 Mbf_sb[:, g * 128:(g + 1) * 128],
                                 wldb[:, slot * 512:(slot + 1) * 512],
                                 start=True, stop=True), mark=True)
                    wt("a", st_, vmk)
                    vaa = act(lambda a, bk=bk: a.copy(
                        ebuf[:, 0:512], bk[:, 0:512]), mark=True)
                    dn_used_a = vaa
                    wt("y", sa, vaa)
                    spv = dma(cSP,
                              wcon_sp[g * 128:(g + 1) * 128, d0:d0 + 512],
                              ebuf[:, 0:512])
                    wt("a", cSP, spv)
                    wt("p", cSP, spv)
        barrier()

        CURPHASE[0] = 1
        with contextlib.ExitStack() as P1:
            winT_sb = sbuf("winT_sb", [128, 8 * 2 * DI], BF16, P1)
            wld1 = sbuf("wld1", [128, 2 * 2048], F32, P1)
            x_tok = sbuf("x_tok", [128, 4 * D], F32, P1)
            xn_tok = sbuf("xn_tok", [128, 4 * D], F32, P1)
            h_tok = sbuf("h_tok", [128, 4 * D], F32, P1)
            tok_bf = sbuf("tok_bf", [128, 4 * D], BF16, P1)
            hfm_sb = sbuf("hfm_sb", [128, 8 * TC], BF16, P1)
            xnfm_sb = sbuf("xnfm_sb", [128, 8 * TC], BF16, P1)
            wn_sb = sbuf("wn_sb", [128, D], F32, P1)
            bn_sb = sbuf("bn_sb", [128, D], F32, P1)
            stat = sbuf("stat", [128, 24], F32, P1)
            junk = sbuf("junk", [128, D], F32, P1)
            ev1 = sbuf("ev1", [128, 4 * 512], F32, P1)

            dma(cLD, wn_sb[:], wn_rep)
            dma(cLD, bn_sb[:], bn_rep)
            vv = 0
            wslot1 = [0, 0]
            for kd in range(8):
                for hcol in range(2):
                    slot = (2 * kd + hcol) % 2
                    wt("y", sv, wslot1[slot])
                    lvv = dma(cLD, wld1[:, slot * 2048:(slot + 1) * 2048],
                              winT[kd * 128:(kd + 1) * 128,
                                   hcol * 2048:(hcol + 1) * 2048])
                    wt("v", cLD, lvv)
                    vv = vec(lambda e, kd=kd, slot=slot, hcol=hcol:
                             e.tensor_copy(
                                 winT_sb[:, kd * 4096 + hcol * 2048:
                                         kd * 4096 + (hcol + 1) * 2048],
                                 wld1[:, slot * 2048:(slot + 1) * 2048]),
                             mark=True)
                    wslot1[slot] = vv
            wt("p", sv, vv)

            ev_done = [0, 0, 0, 0]
            x_used = 0
            xn_spilled = 0
            for c in range(NCH):
                CURPHASE[0] = 111
                t0 = c * TC
                wt("y", sv, x_used)
                wt("v", cSP, xn_spilled)
                lvv = dma(cLD,
                          x_tok.ap().rearrange("p (k f) -> p k f", f=D),
                          x_hbm.rearrange("(k p) f -> p k f", p=128)[
                              :, 4 * c:4 * c + 4, :])
                wt("a", cLD, lvv)
                wt("v", cLD, lvv)
                CURPHASE[0] = 112
                for k in range(4):
                    act(lambda a, k=k: a.activation(
                        junk[:], x_tok[:, k * D:(k + 1) * D], AF.Identity,
                        **({} if noaccum else
                           dict(accum_out=stat[:, k:k + 1]))))
                    vaa = act(lambda a, k=k: a.activation(
                        junk[:], x_tok[:, k * D:(k + 1) * D], AF.Square,
                        **({} if noaccum else
                           dict(accum_out=stat[:, 4 + k:5 + k]))),
                        mark=(k == 3))
                wt("v", sa, vaa)
                CURPHASE[0] = 113
                vec(lambda e: e.tensor_scalar_mul(stat[:, 0:4],
                                                  stat[:, 0:4], REC))
                vec(lambda e: e.tensor_scalar_mul(stat[:, 4:8],
                                                  stat[:, 4:8], REC))
                vec(lambda e: e.tensor_mul(stat[:, 16:20], stat[:, 0:4],
                                           stat[:, 0:4]))
                vvv = vec(lambda e: e.tensor_sub(stat[:, 4:8], stat[:, 4:8],
                                                 stat[:, 16:20]), mark=True)
                vec(lambda e: e.tensor_scalar_add(stat[:, 4:8],
                                                  stat[:, 4:8], EPS))
                vvv = vec(lambda e: e.engine_nop(), mark=True)
                wt("a", sv, vvv)
                vaa = act(lambda a: a.activation(
                    stat[:, 4:8], stat[:, 4:8], AF.Sqrt), mark=True)
                wt("v", sa, vaa)
                vec(lambda e: e.reciprocal(stat[:, 4:8], stat[:, 4:8]))
                for k in range(4):
                    vec(lambda e, k=k: e.tensor_scalar(
                        xn_tok[:, k * D:(k + 1) * D],
                        x_tok[:, k * D:(k + 1) * D],
                        stat[:, k:k + 1], stat[:, 4 + k:5 + k],
                        ALU.subtract, ALU.mult))
                vvv = vec(lambda e: e.engine_nop(), mark=True)
                wt("a", sv, vvv)
                CURPHASE[0] = 114
                for k in range(4):
                    act(lambda a, k=k: a.activation(
                        junk[:], xn_tok[:, k * D:(k + 1) * D], AF.Identity,
                        **({} if noaccum else
                           dict(accum_out=stat[:, 8 + k:9 + k]))))
                    vaa = act(lambda a, k=k: a.activation(
                        junk[:], xn_tok[:, k * D:(k + 1) * D], AF.Square,
                        **({} if noaccum else
                           dict(accum_out=stat[:, 12 + k:13 + k]))),
                        mark=(k == 3))
                wt("v", sa, vaa)
                vec(lambda e: e.tensor_scalar_mul(stat[:, 8:12],
                                                  stat[:, 8:12], REC))
                vec(lambda e: e.tensor_scalar_mul(stat[:, 12:16],
                                                  stat[:, 12:16], REC))
                vec(lambda e: e.tensor_mul(stat[:, 20:24], stat[:, 8:12],
                                           stat[:, 8:12]))
                vvv = vec(lambda e: e.tensor_sub(
                    stat[:, 12:16], stat[:, 12:16], stat[:, 20:24]),
                    mark=True)
                vec(lambda e: e.tensor_scalar_add(stat[:, 12:16],
                                                  stat[:, 12:16], EPS))
                vvv = vec(lambda e: e.engine_nop(), mark=True)
                wt("a", sv, vvv)
                vaa = act(lambda a: a.activation(
                    stat[:, 12:16], stat[:, 12:16], AF.Sqrt), mark=True)
                wt("v", sa, vaa)
                vec(lambda e: e.reciprocal(stat[:, 12:16], stat[:, 12:16]))
                for k in range(4):
                    vec(lambda e, k=k: e.tensor_scalar(
                        h_tok[:, k * D:(k + 1) * D],
                        xn_tok[:, k * D:(k + 1) * D],
                        stat[:, 8 + k:9 + k], stat[:, 12 + k:13 + k],
                        ALU.subtract, ALU.mult))
                    vec(lambda e, k=k: e.tensor_mul(
                        h_tok[:, k * D:(k + 1) * D],
                        h_tok[:, k * D:(k + 1) * D], wn_sb[:]))
                    vec(lambda e, k=k: e.tensor_add(
                        h_tok[:, k * D:(k + 1) * D],
                        h_tok[:, k * D:(k + 1) * D], bn_sb[:]))
                CURPHASE[0] = 115
                vvv = vec(lambda e: e.engine_nop(), mark=True)
                wt("y", sv, vvv)
                xn_spilled = dma(cSP,
                    xn_sp.rearrange("(k p) f -> p k f", p=128)[
                        :, 4 * c:4 * c + 4, :],
                    xn_tok.ap().rearrange("p (k f) -> p k f", f=D))
                CURPHASE[0] = 12
                va2 = None
                for (src, dstfm) in ((xn_tok, xnfm_sb), (h_tok, hfm_sb)):
                    vvv = vec(lambda e, src=src: e.tensor_copy(
                        tok_bf[:], src[:]), mark=True)
                    wt("p", sv, vvv)
                    for j in range(8):
                        bk = psum_bf[j % 2]
                        vmk = None
                        for k in range(4):
                            vmk = pe(lambda e, j=j, k=k, bk=bk:
                                     nc.tensor.transpose(
                                         bk[:, k * 128:(k + 1) * 128],
                                         tok_bf[:, k * D + j * 128:
                                                k * D + (j + 1) * 128],
                                         ident[:]), mark=True)
                        wt("a", st_, vmk)
                        va2 = act(lambda a, j=j, bk=bk, dstfm=dstfm: a.copy(
                            dstfm[:, j * TC:(j + 1) * TC], bk[:, 0:512]),
                            mark=True)
                        wt("p", sa, va2)
                x_used = vvv
                wt("y", sa, va2)
                dma(cSP,
                    xnfm_sp.rearrange("(j p) t -> p j t", p=128)[
                        :, :, t0:t0 + TC],
                    xnfm_sb.ap().rearrange("p (j t) -> p j t", t=TC))
                CURPHASE[0] = 13
                for et in range(32):
                    bk = psum[et % 4]
                    wt("p", sa, ev_done[et % 4])
                    vmk = None
                    for kd in range(8):
                        vmk = pe(lambda e, et=et, kd=kd, bk=bk:
                                 nc.tensor.matmul(
                                     bk[:, 0:512],
                                     winT_sb[:, kd * 4096 + et * 128:
                                             kd * 4096 + (et + 1) * 128],
                                     hfm_sb[:, kd * TC:(kd + 1) * TC],
                                     start=(kd == 0), stop=(kd == 7)),
                                 mark=True)
                    wt("a", st_, vmk)
                    vaa = act(lambda a, et=et, bk=bk: a.copy(
                        ev1[:, (et % 4) * 512:(et % 4 + 1) * 512],
                        bk[:, 0:512]), mark=True)
                    ev_done[et % 4] = vaa
                    wt("y", sa, vaa)
                    dma(cSP, xz_sp[et * 128:(et + 1) * 128, t0:t0 + TC],
                        ev1[:, (et % 4) * 512:(et % 4 + 1) * 512])
                wt("a", cSP, cSP.n)
        barrier()

        CURPHASE[0] = 2
        with contextlib.ExitStack() as P23:
            xc = sbuf("xc", [128, 2 * (TC + 3)], F32, P23)
            xrv = sbuf("xrv", [128, TC + 3], F32, P23)
            u_all = sbuf("u_all", [128, NDT * TC], F32, P23)
            u_bf = sbuf("u_bf", [128, 2 * TC], BF16, P23)
            xdbl = sbuf("xdbl", [128, TC], F32, P23)
            dtl_bf = sbuf("dtl_bf", [64, TC], BF16, P23)
            dt_sb = sbuf("dt_sb", [128, TC], F32, P23)
            dtu_sb = sbuf("dtu_sb", [128, TC], F32, P23)
            sp_t = sbuf("sp_t", [128, TC], F32, P23)
            sp_r = sbuf("sp_r", [128, TC], F32, P23)
            brep = sbuf("brep", [128, NST * TC], F32, P23)
            crep = sbuf("crep", [128, NST * TC], F32, P23)
            da_sb = sbuf("da_sb", [128, 8 * TC], F32, P23)
            b_sb = sbuf("b_sb", [128, TC], F32, P23)
            h_sb = sbuf("h_sb", [128, TC], F32, P23)
            m_sb = sbuf("m_sb", [128, TC], F32, P23)
            y_dt = sbuf("y_dt", [128, 2 * TC], F32, P23)
            ybf_all = sbuf("ybf_all", [128, NDT * TC], BF16, P23)
            yf_ld = sbuf("yf_ld", [128, TC], F32, P23)
            zc_sb = sbuf("zc_sb", [128, TC], F32, P23)
            g_sb = sbuf("g_sb", [128, TC], F32, P23)
            wout_sb = sbuf("wout_sb", [128, NDT * 512], BF16, P23)
            wld2 = sbuf("wld2", [128, 1024], F32, P23)
            xpw_sb = {s: sbuf(f"xpw_sb_{s}", [128, NDT * 96], BF16, P23)
                      for s in "fr"}
            dtw_sb = {s: sbuf(f"dtw_sb_{s}", [64, NDT * 128], BF16, P23)
                      for s in "fr"}
            att_ev = sbuf("att_ev", [128, 512], F32, P23)
            att_tok = sbuf("att_tok", [128, 512], F32, P23)

            vv = None
            for s in "fr":
                for xh in range(2):
                    wt("y", sv, vv)
                    lvv = dma(cLD,
                              wld2[:, 0:768].rearrange(
                                  "p (dt c) -> p dt c", c=96),
                              xpwT_in[s].rearrange(
                                  "(dt p) c -> p dt c", p=128)[
                                  :, xh * 8:(xh + 1) * 8, :])
                    wt("v", cLD, lvv)
                    vv = vec(lambda e, s=s, xh=xh: e.tensor_copy(
                        xpw_sb[s][:, xh * 768:(xh + 1) * 768],
                        wld2[:, 0:768]), mark=True)
                for dh in range(2):
                    wt("y", sv, vv)
                    lvv = dma(cLD, wld2[0:64, 0:1024],
                              dtwT_in[s][:, dh * 1024:(dh + 1) * 1024])
                    wt("v", cLD, lvv)
                    vv = vec(lambda e, s=s, dh=dh: e.tensor_copy(
                        dtw_sb[s][0:64, dh * 1024:(dh + 1) * 1024],
                        wld2[0:64, 0:1024]), mark=True)
            wt("p", sv, vv)

            def mamba_pass(bk_, rev):
                dbg = rev and ("ysum_sp" in taps)
                vec(lambda e: e.memset(carry_sb[:], 0.0))
                vec(lambda e: e.engine_nop(), mark=True)
                conv_done = [0, 0]
                wout_guard = [0]
                att_done = [0]
                xrv_free = [0]
                last_dtmm = [0]
                grp_used = [0, 0]
                ucast_last = [0]
                uprojmm = [0, 0]
                dtmm_clear = [0, 0]
                yspill = [0, 0]
                xz_evac_prev = 0
                bc_read_guard = 0
                loopB_done = 0
                prev_vsc = 0
                for c in range(NCH):
                    t0 = c * TC
                    co = NCH - 1 - c
                    vmk = None
                    wt("v", sa, ucast_last[0])
                    for dt_i in range(NDT):
                        slot = dt_i % 2
                        xcs = xc[:, slot * (TC + 3):(slot + 1) * (TC + 3)]
                        row0 = dt_i * 128
                        wt("y", sv, conv_done[slot])
                        if not rev:
                            if c == 0:
                                lvv = dma(cLD, xcs[:, 3:TC + 3],
                                          xz_sp[row0:row0 + 128, 0:TC])
                            else:
                                lvv = dma(cLD, xcs[:, 0:TC + 3],
                                          xz_sp[row0:row0 + 128,
                                                t0 - 3:t0 + TC])
                        else:
                            wt("y", sv, xrv_free[0])
                            if c == 0:
                                lvv = dma(cLD, xrv[:, 0:TC],
                                          xz_sp[row0:row0 + 128, L - TC:L])
                            else:
                                lvv = dma(cLD, xrv[:, 0:TC + 3],
                                          xz_sp[row0:row0 + 128,
                                                L - t0 - TC:L - t0 + 3])
                        wt("v", cLD, lvv)
                        if rev:
                            if c == 0:
                                vec(lambda e, xcs=xcs: e.tensor_copy(
                                    xcs[:, 3:TC + 3], xrv[:, TC - 1::-1]))
                            else:
                                vec(lambda e, xcs=xcs: e.tensor_copy(
                                    xcs[:, 0:TC + 3], xrv[:, TC + 2::-1]))
                            xrv_free[0] = sv.n + 1
                        if c == 0:
                            vec(lambda e, xcs=xcs: e.memset(xcs[:, 0:3], 0.0))
                        ua = u_all[:, dt_i * TC:(dt_i + 1) * TC]
                        vec(lambda e, ua=ua, xcs=xcs, dt_i=dt_i:
                            e.tensor_scalar(
                                ua, xcs[:, 0:TC],
                                cw_sb[bk_][:, dt_i * 4:dt_i * 4 + 1], None,
                                ALU.mult))
                        for k in range(1, 4):
                            vec(lambda e, ua=ua, xcs=xcs, dt_i=dt_i, k=k:
                                e.scalar_tensor_tensor(
                                    ua, xcs[:, k:k + TC],
                                    cw_sb[bk_][:, dt_i * 4 + k:
                                               dt_i * 4 + k + 1],
                                    ua, ALU.mult, ALU.add))
                        conv_done[slot] = vec(lambda e: e.engine_nop(),
                                              mark=True)
                        wt("a", sv, conv_done[slot])
                        vaa = act(lambda a, ua=ua, dt_i=dt_i: a.activation(
                            ua, ua, AF.Silu,
                            bias=cb_sb[bk_][:, dt_i:dt_i + 1]), mark=True)
                        wt("v", sa, vaa)
                        uslot = dt_i % 2
                        wt("a", st_, uprojmm[uslot])
                        vaa2 = act(lambda a, ua=ua, uslot=uslot: a.activation(
                            u_bf[:, uslot * TC:(uslot + 1) * TC], ua,
                            AF.Identity), mark=True)
                        wt("p", sa, vaa2)
                        if dt_i == NDT - 1:
                            ucast_last[0] = vaa2
                        if dt_i == 0:
                            wt("p", sa, xz_evac_prev)
                            wt("p", sa, att_done[0])
                        vmk = pe(lambda e, dt_i=dt_i, uslot=uslot:
                                 nc.tensor.matmul(
                                     psum[4][0:96, 0:TC],
                                     xpw_sb[bk_][:, dt_i * 96:
                                                 (dt_i + 1) * 96],
                                     u_bf[:, uslot * TC:(uslot + 1) * TC],
                                     start=(dt_i == 0),
                                     stop=(dt_i == NDT - 1)), mark=True)
                        uprojmm[uslot] = vmk
                    wt("a", st_, vmk)
                    wt("a", cSP, cSP.n)
                    vaa = act(lambda a: a.copy(xdbl[0:96, :],
                                               psum[4][0:96, 0:TC]),
                              mark=True)
                    xz_evac_prev = vaa
                    wt("y", sa, vaa)
                    slot_bc = c % 2
                    spv = dma(cSP, bc_sp[slot_bc], xdbl[64:96, :])
                    wt("y", cSP, spv)
                    wt("y", sv, bc_read_guard)
                    bcp = []
                    for pc_ in range(4):
                        dma(cBC, brep[:, pc_ * 4 * TC:(pc_ + 1) * 4 * TC],
                            bc_sp[slot_bc, pc_ * 4:(pc_ + 1) * 4,
                                  :].partition_broadcast(128))
                        bcp.append(dma(
                            cBC, crep[:, pc_ * 4 * TC:(pc_ + 1) * 4 * TC],
                            bc_sp[slot_bc, NST + pc_ * 4:NST + (pc_ + 1) * 4,
                                  :].partition_broadcast(128)))
                    bcv = bcp[3]
                    wt("a", st_, last_dtmm[0])
                    vaa2 = act(lambda a: a.activation(
                        dtl_bf[0:64, :], xdbl[0:64, :], AF.Identity),
                        mark=True)
                    wt("p", sa, vaa2)
                    if rev:
                        wt("v", st_, loopB_done)
                    vsc = None
                    for dt_i in range(NDT):
                        bk2 = psum[[5, 3][dt_i % 2]]
                        wt("p", sv, dtmm_clear[dt_i % 2])
                        wt("p", sa, att_done[0])
                        vmk = pe(lambda e, dt_i=dt_i, bk2=bk2:
                                 nc.tensor.matmul(
                                     bk2[:, 0:TC],
                                     dtw_sb[bk_][0:64,
                                                 dt_i * 128:(dt_i + 1) * 128],
                                     dtl_bf[0:64, :], start=True, stop=True),
                                 mark=True)
                        last_dtmm[0] = vmk
                        wt("a", st_, vmk)
                        wt("v", st_, vmk)
                        vab = act(lambda a, dt_i=dt_i, bk2=bk2: a.activation(
                            sp_t[:], bk2[:, 0:TC], AF.Abs,
                            bias=dtb_sb[bk_][:, dt_i:dt_i + 1]), mark=True)
                        wt("v", sa, vab)
                        vvv = vec(lambda e, dt_i=dt_i, bk2=bk2:
                                  e.tensor_scalar(
                                      sp_r[:], bk2[:, 0:TC],
                                      dtb_sb[bk_][:, dt_i:dt_i + 1], 0.0,
                                      ALU.add, ALU.max), mark=True)
                        dtmm_clear[dt_i % 2] = vvv
                        act(lambda a: a.activation(sp_t[:], sp_t[:], AF.Exp,
                                                   scale=-1.0))
                        vaa = act(lambda a: a.activation(
                            sp_t[:], sp_t[:], AF.Ln, bias=1.0), mark=True)
                        wt("v", sa, vaa)
                        vec(lambda e: e.tensor_add(dt_sb[:], sp_t[:],
                                                   sp_r[:]))
                        ua = u_all[:, dt_i * TC:(dt_i + 1) * TC]
                        vdt = vec(lambda e, ua=ua: e.tensor_mul(
                            dtu_sb[:], dt_sb[:], ua), mark=True)
                        wt("a", sv, vdt)
                        for grp in range(4):
                            wt("v", cBC, bcp[grp])
                            gslot = grp % 2
                            gbase = gslot * 4 * TC
                            wt("a", sv, grp_used[gslot])
                            vaa = None
                            for j in range(4):
                                n = grp * 4 + j
                                vaa = act(
                                    lambda a, n=n, dt_i=dt_i, gbase=gbase,
                                    j=j: a.activation(
                                        da_sb[:, gbase + j * TC:
                                              gbase + (j + 1) * TC],
                                        dt_sb[:], AF.Exp,
                                        scale=A_sb[bk_][
                                            :, dt_i * NST + n:
                                            dt_i * NST + n + 1]),
                                    mark=(j == 3))
                            wt("v", sa, vaa)
                            for j in range(4):
                                n = grp * 4 + j
                                vec(lambda e, n=n: e.tensor_mul(
                                    b_sb[:], dtu_sb[:],
                                    brep[:, n * TC:(n + 1) * TC]))
                                ci = dt_i * NST + n
                                vec(lambda e, ci=ci, gbase=gbase, j=j:
                                    e.tensor_tensor_scan(
                                        h_sb[:],
                                        da_sb[:, gbase + j * TC:
                                              gbase + (j + 1) * TC],
                                        b_sb[:], carry_sb[:, ci:ci + 1],
                                        ALU.mult, ALU.add))
                                vec(lambda e, ci=ci: e.tensor_copy(
                                    carry_sb[:, ci:ci + 1],
                                    h_sb[:, TC - 1:TC]))
                                yslot = dt_i % 2
                                if n == 0:
                                    vec(lambda e, n=n, yslot=yslot:
                                        e.tensor_mul(
                                            y_dt[:, yslot * TC:
                                                 (yslot + 1) * TC],
                                            h_sb[:],
                                            crep[:, n * TC:(n + 1) * TC]))
                                else:
                                    vec(lambda e, n=n: e.tensor_mul(
                                        m_sb[:], h_sb[:],
                                        crep[:, n * TC:(n + 1) * TC]))
                                    vec(lambda e, yslot=yslot: e.tensor_add(
                                        y_dt[:, yslot * TC:(yslot + 1) * TC],
                                        y_dt[:, yslot * TC:(yslot + 1) * TC],
                                        m_sb[:]))
                            grp_used[gslot] = vec(
                                lambda e: e.engine_nop(), mark=True)
                        bc_read_guard = grp_used[1]
                        yslot = dt_i % 2
                        ys = y_dt[:, yslot * TC:(yslot + 1) * TC]
                        if not rev:
                            wt("v", cSP, yspill[yslot])
                            vsc = vec(lambda e, ys=ys, ua=ua, dt_i=dt_i:
                                      e.scalar_tensor_tensor(
                                          ys, ua,
                                          Dp_sb[bk_][:, dt_i:dt_i + 1],
                                          ys, ALU.mult, ALU.add), mark=True)
                            wt("y", sv, vsc)
                            yspill[yslot] = dma(
                                cSP, yf_sp[dt_i * 128:(dt_i + 1) * 128,
                                           t0:t0 + TC], ys)
                        else:
                            vsc = vec(lambda e, ys=ys, ua=ua, dt_i=dt_i:
                                      e.scalar_tensor_tensor(
                                          ys, ua,
                                          Dp_sb[bk_][:, dt_i:dt_i + 1],
                                          ys, ALU.mult, ALU.add), mark=True)
                            row0 = dt_i * 128
                            wt("y", sv, vsc)
                            if dbg:
                                dma(cSP, yr_sp[row0:row0 + 128,
                                               t0:t0 + TC], ys)
                            dma(cLD, yf_ld[:],
                                yf_sp[row0:row0 + 128, co * TC:(co + 1) * TC])
                            lv2 = dma(cLD, zc_sb[:],
                                      xz_sp[DI + row0:DI + row0 + 128,
                                            co * TC:(co + 1) * TC])
                            wt("a", cLD, lv2)
                            wt("a", sv, prev_vsc)
                            vaa = act(lambda a: a.activation(
                                g_sb[:], zc_sb[:], AF.Silu), mark=True)
                            wt("v", cLD, lv2)
                            wt("v", sa, vaa)
                            vec(lambda e, ys=ys: e.tensor_add(
                                yf_ld[:], yf_ld[:], ys[:, ::-1]))
                            vec(lambda e: e.tensor_mul(
                                yf_ld[:], yf_ld[:], g_sb[:]))
                            vsc = vec(lambda e, dt_i=dt_i: e.tensor_copy(
                                ybf_all[:, dt_i * TC:(dt_i + 1) * TC],
                                yf_ld[:]), mark=True)
                            prev_vsc = vsc
                            if dbg:
                                wt("y", sv, vsc)
                                dma(cSP, ysum_sp[row0:row0 + 128,
                                                 co * TC:(co + 1) * TC],
                                    yf_ld[:])
                    if rev:
                        for oh in range(2):
                            vv2 = None
                            wt("y", st_, wout_guard[0])
                            wslot_used = [0, 0]
                            for dt_i in range(NDT):
                                ws = dt_i % 2
                                wt("y", sa, wslot_used[ws])
                                lvv = dma(cLD,
                                          wld2[:, ws * 512:(ws + 1) * 512],
                                          woutT.rearrange(
                                              "(dt p) o -> p dt o", p=128)[
                                              :, dt_i,
                                              oh * 512:(oh + 1) * 512])
                                wt("a", cLD, lvv)
                                vv2 = act(lambda a, dt_i=dt_i, ws=ws:
                                          a.activation(
                                              wout_sb[:, dt_i * 512:
                                                      (dt_i + 1) * 512],
                                              wld2[:, ws * 512:
                                                   (ws + 1) * 512],
                                              AF.Identity),
                                          mark=True)
                                wslot_used[ws] = vv2
                            wt("p", sa, vv2)
                            wt("p", sv, vsc)
                            for ot in range(4):
                                bk2 = psum[ot]
                                if ot == 3:
                                    wt("p", sv, dtmm_clear[0])
                                    wt("p", sv, dtmm_clear[1])
                                vmk = None
                                for dt_i in range(NDT):
                                    vmk = pe(lambda e, ot=ot, dt_i=dt_i,
                                             bk2=bk2: nc.tensor.matmul(
                                        bk2[:, 0:512],
                                        wout_sb[:, dt_i * 512 + ot * 128:
                                                dt_i * 512 +
                                                (ot + 1) * 128],
                                        ybf_all[:, dt_i * TC:
                                                (dt_i + 1) * TC],
                                        start=(dt_i == 0),
                                        stop=(dt_i == NDT - 1)), mark=True)
                                if ot == 3:
                                    wout_guard[0] = vmk
                                    if oh == 1:
                                        loopB_done = vmk
                                wt("a", st_, vmk)
                                vaa = act(lambda a, bk2=bk2: a.copy(
                                    att_ev[:], bk2[:, 0:512]), mark=True)
                                wt("p", sa, vaa)
                                o_ix = oh * 4 + ot
                                va3 = None
                                for k in range(4):
                                    bk3 = psum[4 + k % 2]
                                    wt("p", sa, att_done[0] if k < 2 else 0)
                                    vmk = pe(lambda e, k=k, bk3=bk3:
                                             nc.tensor.transpose(
                                                 bk3[:, 0:128],
                                                 att_ev[:, k * 128:
                                                        (k + 1) * 128],
                                                 ident_f[:]), mark=True)
                                    wt("a", st_, vmk)
                                    va3 = act(lambda a, k=k, bk3=bk3:
                                              a.copy(
                                        att_tok[:, k * 128:(k + 1) * 128],
                                        bk3[:, 0:128]), mark=True)
                                    wt("p", sa, va3)
                                att_done[0] = va3
                                wt("y", sa, va3)
                                dma(cSP,
                                    attn_sp.rearrange(
                                        "(k p) f -> p k f", p=128)[
                                        :, 4 * co:4 * co + 4,
                                        o_ix * 128:(o_ix + 1) * 128],
                                    att_tok[:, 0:512].rearrange(
                                        "p (k f) -> p k f", f=128))
                                wt("a", cSP, cSP.n)

            mamba_pass("f", rev=False)
            barrier()
            CURPHASE[0] = 3
            mamba_pass("r", rev=True)
        barrier()

        CURPHASE[0] = 4
        with contextlib.ExitStack() as P4:
            weff_sb = sbuf("weff_sb", [128, 8 * DH], BF16, P4)
            wcon_sb = sbuf("wcon_sb", [128, 32 * 512], BF16, P4)
            xnfm_c = sbuf("xnfm_c", [128, 8 * TC], BF16, P4)
            h1_bf = sbuf("h1_bf", [128, 32 * TC], BF16, P4)
            ffn_f = sbuf("ffn_f", [128, 8 * 512], F32, P4)
            acc_tok = sbuf("acc_tok", [128, 2 * D], F32, P4)
            ld_tok = sbuf("ld_tok", [128, 2 * D], F32, P4)

            lvv = dma(cLD,
                      weff_sb.ap().rearrange("p (kd f) -> p kd f", f=DH),
                      weff_sp.rearrange("(kd p) f -> p kd f", p=128))
            wt("p", cLD, lvv)
            xn_guard = 0
            wcon_guard = 0
            ffn_guard = 0
            acc_spill = [0, 0]
            h1ev = [0, 0, 0, 0]
            h2ev = [0, 0]
            tr_guard = [0, 0]
            for c in range(NCH):
                t0 = c * TC
                wt("y", st_, xn_guard)
                lvx = dma(cLD, xnfm_c[:],
                          xnfm_sp.rearrange("(j p) t -> p j t", p=128)[
                              :, :, t0:t0 + TC])
                wt("p", cLD, lvx)
                vaa = None
                for ft in range(32):
                    bki = ft % 4
                    bk = psum[bki]
                    wt("p", sa, h1ev[bki])
                    if ft < 2:
                        wt("p", sv, tr_guard[ft % 2])
                    vmk = None
                    for kd in range(8):
                        vmk = pe(lambda e, ft=ft, kd=kd, bk=bk:
                                 nc.tensor.matmul(
                                     bk[:, 0:512],
                                     weff_sb[:, kd * DH + ft * 128:
                                             kd * DH + (ft + 1) * 128],
                                     xnfm_c[:, kd * TC:(kd + 1) * TC],
                                     start=(kd == 0), stop=(kd == 7)),
                                 mark=True)
                    if ft == 31:
                        xn_guard = vmk
                    wt("a", st_, vmk)
                    vaa = act(lambda a, ft=ft, bk=bk: a.activation(
                        h1_bf[:, ft * TC:(ft + 1) * TC], bk[:, 0:512],
                        AF.Gelu, bias=bias1_sb[:, ft:ft + 1]), mark=True)
                    h1ev[bki] = vaa
                v_h1 = vaa
                vv2 = None
                for oh in range(2):
                    wt("y", st_, wcon_guard)
                    lvw = dma(cLD,
                              wcon_sb.ap().rearrange(
                                  "p (kf o) -> p kf o", o=512),
                              wcon_sp.rearrange("(kf p) o -> p kf o", p=128)[
                                  :, :, oh * 512:(oh + 1) * 512])
                    wt("p", cLD, lvw)
                    if oh == 0:
                        wt("p", sa, v_h1)
                    for ot in range(4):
                        bki = 4 + ot % 2
                        bk = psum[bki]
                        wt("p", sa, h2ev[ot % 2])
                        vmk = None
                        for kf in range(32):
                            vmk = pe(lambda e, ot=ot, kf=kf, bk=bk, oh=oh:
                                     nc.tensor.matmul(
                                         bk[:, 0:512],
                                         wcon_sb[:, kf * 512 + ot * 128:
                                                 kf * 512 + (ot + 1) * 128],
                                         h1_bf[:, kf * TC:(kf + 1) * TC],
                                         start=(kf == 0), stop=(kf == 31)),
                                     mark=True)
                        if ot == 3:
                            wcon_guard = vmk
                        wt("a", st_, vmk)
                        o_ix = oh * 4 + ot
                        wt("a", st_, ffn_guard)
                        vaa = act(lambda a, bk=bk, o_ix=o_ix: a.activation(
                            ffn_f[:, o_ix * 512:(o_ix + 1) * 512],
                            bk[:, 0:512], AF.Identity,
                            bias=bias2_sb[:, o_ix:o_ix + 1]), mark=True)
                        h2ev[ot % 2] = vaa
                        vv2 = vaa
                v_ffn = vv2
                for k in range(4):
                    aslot = k % 2
                    accs = acc_tok[:, aslot * D:(aslot + 1) * D]
                    lds = ld_tok[:, aslot * D:(aslot + 1) * D]
                    r0 = t0 + k * 128
                    wt("y", cSP, acc_spill[aslot])
                    lv1 = dma(cLD, accs, x_hbm[r0:r0 + 128, :])
                    lv1 = dma(cLD, lds, xn_sp[r0:r0 + 128, :])
                    wt("v", cLD, lv1)
                    vv3 = vec(lambda e, accs=accs, lds=lds: e.tensor_add(
                        accs, accs, lds), mark=True)
                    wt("y", sv, vv3)
                    lv1 = dma(cLD, lds, attn_sp[r0:r0 + 128, :])
                    wt("v", cLD, lv1)
                    vv3 = vec(lambda e, accs=accs, lds=lds: e.tensor_add(
                        accs, accs, lds), mark=True)
                    wt("p", sa, v_ffn)
                    for ob in range(8):
                        bk3 = psum[ob % 2]
                        wt("p", sv, tr_guard[ob % 2])
                        vmk = pe(lambda e, ob=ob, k=k, bk3=bk3:
                                 nc.tensor.transpose(
                                     bk3[:, 0:128],
                                     ffn_f[:, ob * 512 + k * 128:
                                           ob * 512 + (k + 1) * 128],
                                     ident_f[:]), mark=True)
                        ffn_guard = vmk
                        wt("v", st_, vmk)
                        vv3 = vec(lambda e, ob=ob, bk3=bk3, accs=accs:
                                  e.tensor_add(
                                      accs[:, ob * 128:(ob + 1) * 128],
                                      accs[:, ob * 128:(ob + 1) * 128],
                                      bk3[:, 0:128]), mark=True)
                        tr_guard[ob % 2] = vv3
                    wt("y", sv, vv3)
                    acc_spill[aslot] = dma(cSP, out_hbm[r0:r0 + 128, :], accs)
        on("y", lambda e: e.wait_ge(cSP.h, cSP.n))
        on("y", lambda e: e.wait_ge(cLD.h, cLD.n))
        on("y", lambda e: e.wait_ge(cBC.h, cBC.n))

        blk = CTX.enter_context(nc.Block())
        for ek, attr in (("a", "scalar"), ("v", "vector"), ("p", "tensor"),
                         ("g", "gpsimd"), ("y", "sync")):
            ops = [fn for k, fn in PROG if k == ek]

            def runner(eng, ops=ops):
                for fn in ops:
                    fn(eng)
            getattr(blk, attr)(runner)
    return nc


def _coeff_tables(w):
    f = np.arange(DH)
    S = np.empty((NSTG, DH), np.float32)
    X = np.empty((NSTG, DH), np.float32)
    XT = np.empty((NSTG, DH), np.float32)
    for s in range(NSTG):
        stp = 2 ** s
        ws = w[s].reshape(DH // (2 * stp), stp, 2, 2)
        blk = f // (2 * stp)
        half = (f // stp) % 2
        off = f % stp
        S[s] = ws[blk, off, half, half]
        X[s] = ws[blk, off, half, 1 - half]
        XT[s] = ws[blk, off, 1 - half, half]

    def colform(A):
        return np.ascontiguousarray(
            A.reshape(NSTG, 32, 128).transpose(0, 2, 1))
    return (np.ascontiguousarray(S), np.ascontiguousarray(X),
            np.ascontiguousarray(XT), colform(S), colform(X), colform(XT))


def _prep_inputs(inputs):
    f32 = np.float32
    c = {}
    x = np.asarray(inputs["L0_attn_x"], f32)
    c["wn_rep"] = np.ascontiguousarray(np.broadcast_to(
        np.asarray(inputs["L0_attn_norm_w"], f32), (128, D)))
    c["bn_rep"] = np.ascontiguousarray(np.broadcast_to(
        np.asarray(inputs["L0_attn_norm_b"], f32), (128, D)))
    c["winT"] = np.ascontiguousarray(
        np.asarray(inputs["L0_attn_in_proj_w"], f32).T)
    c["woutT"] = np.ascontiguousarray(
        np.asarray(inputs["L0_attn_out_proj_w"], f32).T)
    c["wexpT"] = np.ascontiguousarray(np.asarray(inputs["L0_ffn_W_exp"], f32).T)
    c["wcon"] = np.ascontiguousarray(np.asarray(inputs["L0_ffn_W_con"], f32))
    upw = np.asarray(inputs["L0_ffn_up_weights"], f32)
    dnw = np.asarray(inputs["L0_ffn_down_weights"], f32)
    uS, uX, uXT, uSc, uXc, uXTc = _coeff_tables(upw)
    dS, dX, dXT, dSc, dXc, dXTc = _coeff_tables(dnw)
    c["upS_row"], c["upX_row"] = uS, uX
    c["dnS_row"], c["dnXT_row"] = dS, dXT
    c["upS_col"], c["upX_col"] = uSc, uXc
    c["dnS_col"], c["dnXT_col"] = dSc, dXTc
    c["bexp_row"] = np.ascontiguousarray(
        np.asarray(inputs["L0_ffn_b_exp"], f32).reshape(1, DH))
    c["upb_col"] = np.ascontiguousarray(
        np.asarray(inputs["L0_ffn_up_bias"], f32).reshape(32, 128).T)
    c["bcon_col"] = np.ascontiguousarray(
        np.asarray(inputs["L0_ffn_b_con"], f32).reshape(8, 128).T)
    c["dnb_col"] = np.ascontiguousarray(
        np.asarray(inputs["L0_ffn_down_bias"], f32).reshape(8, 128).T)
    for s in ("f", "r"):
        A = np.asarray(inputs[f"L0_attn_A_log_{s}"], f32)
        c[f"A_tab_{s}"] = np.ascontiguousarray(
            A.reshape(NDT, 128, NST).transpose(1, 0, 2).reshape(
                128, NDT * NST))
        cw = np.asarray(inputs[f"L0_attn_conv_w_{s}"], f32)[:, 0, :]
        c[f"cw_tab_{s}"] = np.ascontiguousarray(
            cw.reshape(NDT, 128, 4).transpose(1, 0, 2).reshape(128, NDT * 4))
        for nm, key in (("cb", "conv_b"), ("dtb", "dt_proj_b"), ("Dp", "D")):
            v = np.asarray(inputs[f"L0_attn_{key}_{s}"], f32)
            c[f"{nm}_tab_{s}"] = np.ascontiguousarray(v.reshape(NDT, 128).T)
        c[f"xpwT_{s}"] = np.ascontiguousarray(
            np.asarray(inputs[f"L0_attn_x_proj_w_{s}"], f32).T)
        c[f"dtwT_{s}"] = np.ascontiguousarray(
            np.asarray(inputs[f"L0_attn_dt_proj_w_{s}"], f32).T)
    in_maps = []
    for core in range(NCORES):
        m = dict(c)
        m["x"] = np.ascontiguousarray(x[core])
        in_maps.append(m)
    return in_maps


_NC_CACHE = {}


def kernel(**inputs):
    import time as _time
    if "default" not in _NC_CACHE:
        _NC_CACHE["default"] = build()
    nc = _NC_CACHE["default"]
    in_maps = _prep_inputs(inputs)
    last = None
    for attempt in range(3):
        if attempt:
            _time.sleep(30 * attempt)
        try:
            res = run_bass_kernel_spmd(nc, in_maps,
                                       core_ids=list(range(NCORES)))
            out = np.stack([np.asarray(r["out"]) for r in res.results],
                           axis=0)
            return out.astype(np.float32)
        except Exception as ex:
            last = ex
    raise last
